# version 03 (final)
# baseline (speedup 1.0000x reference)
import numpy as np
import ml_dtypes
from contextlib import ExitStack

import concourse.bass as bass
import concourse.mybir as mybir
from concourse.bass_utils import run_bass_kernel_spmd
from concourse import library_config

F32 = mybir.dt.float32
BF16 = mybir.dt.bfloat16
ALU = mybir.AluOpType
ACT = mybir.ActivationFunctionType
AX = mybir.AxisListType.X

P = 128
C = 2048
H = 16
D = 128
M23 = float(1.5 * 2.0 ** 23)
M52 = float(1.5 * 2.0 ** 52)
SM_SCALE = float(D ** -0.5)
EPS1 = 1e-5
EPS2 = 1e-6
B48 = float(2.0 ** 48)

CFG = dict(NCH=16, HOT=64, DBG=0, PHASES=99)


class KB:
    def __init__(self, cfg):
        self.cfg = cfg
        self.NCH = cfg["NCH"]
        self.N = self.NCH * P
        self.WN = min(512, self.N)
        self.NWIN = self.N // self.WN
        self.HOT = cfg["HOT"]
        self.HID = self.HOT * P
        self.DBG = cfg["DBG"]
        self.PHASES = cfg["PHASES"]
        self.nc = bass.Bass("TRN2", target_bir_lowering=False, debug=False)
        self.stk = ExitStack()
        self.cur = self.stk
        self.dbg_outs = {}

    def sem(self, name):
        h = self.stk.enter_context(self.nc.semaphore(name))
        return {"h": h, "n": 0}

    def inc(self, s, instr, k=1):
        instr.then_inc(s["h"], k)
        s["n"] += k
        return (s["h"], s["n"])

    def ev(self, s):
        return (s["h"], s["n"])

    @staticmethod
    def wt(eng, *evs):
        for e in evs:
            if e is not None:
                eng.wait_ge(e[0], e[1])

    def sb(self, name, shape, dtype, scope=None):
        return (scope or self.cur).enter_context(self.nc.sbuf_tensor(name, shape, dtype))

    def ps(self, name, shape, dtype, scope=None):
        return (scope or self.cur).enter_context(self.nc.psum_tensor(name, shape, dtype))

    def dram(self, name, shape, dtype, kind):
        return self.nc.dram_tensor(name, shape, dtype, kind=kind).ap()

    def dbg_out(self, name, shape):
        if name not in self.dbg_outs:
            self.dbg_outs[name] = self.dram("dbg_" + name, shape, F32, "ExternalOutput")
        return self.dbg_outs[name]

    def dgrp(self, eng):
        if "GpSimd" in type(eng).__name__:
            s = self.gp_pool[self.gp_rr]
            self.gp_rr = (self.gp_rr + 1) % len(self.gp_pool)
        else:
            s = self.dma_pool[self.dma_rr]
            self.dma_rr = (self.dma_rr + 1) % len(self.dma_pool)
        if s["n"] > 0:
            eng.wait_ge(s["h"], s["n"])
        return s

    def barrier(self, tag):
        b = self.block
        evs = {}

        @b.tensor
        def _(t):
            self.wt(t, self.idf_ready)
            evs["te"] = self.inc(self.sTE, t.ldweights(self.idb[:, 0:1]))

        @b.vector
        def _(v):
            evs["ve"] = self.inc(self.sVE, v.memset(self.bsv[:], 0.0))

        @b.scalar
        def _(sc):
            evs["se"] = self.inc(self.sSE, sc.memzero(self.bss[:]))

        @b.gpsimd
        def _(g):
            evs["ge"] = self.inc(self.sGE, g.memset(self.bsg[:], 0.0))

        dma_evs = [self.ev(s) for s in self.dma_pool + self.gp_pool if s["n"] > 0]
        all_evs = list(evs.values()) + dma_evs

        @b.tensor
        def _(t):
            self.wt(t, *all_evs)

        @b.vector
        def _(v):
            self.wt(v, *all_evs)

        @b.scalar
        def _(sc):
            self.wt(sc, *all_evs)

        @b.gpsimd
        def _(g):
            self.wt(g, *all_evs)

        @b.sync
        def _(sy):
            self.wt(sy, *all_evs)

    def declare(self):
        d = self.dram
        N, NCH, HID, HOT = self.N, self.NCH, self.HID, self.HOT
        self.x_d = d("x", [2048, C], F32, "ExternalInput")
        self.qkvT_d = d("qkvT", [C, 3 * C], F32, "ExternalInput")
        self.qkvb_d = d("qkvb", [1, 3 * C], F32, "ExternalInput")
        self.gateT_d = d("gateT", [C, C], F32, "ExternalInput")
        self.gateb_d = d("gateb", [1, C], F32, "ExternalInput")
        self.projT_d = d("projT", [C, C], F32, "ExternalInput")
        self.projb_d = d("projb", [1, C], F32, "ExternalInput")
        self.nwrow_d = d("nwrow", [1, C], F32, "ExternalInput")
        self.w1T_d = d("w1T", [C, 8192], F32, "ExternalInput")
        self.w2T_d = d("w2T", [8192, C], F32, "ExternalInput")
        self.garow_d = d("gammarow", [1, C], F32, "ExternalInput")
        self.idb_d = d("identb", [P, P], BF16, "ExternalInput")
        self.idf_d = d("identf", [P, P], F32, "ExternalInput")
        self.mJ_d = d("maskJ", [1, 256], BF16, "ExternalInput")
        self.mB_d = d("maskB", [8, 256], BF16, "ExternalInput")
        self.y_d = d("y", [N, C], F32, "ExternalOutput")
        self.qh_d = d("qh_s", [NCH, H, P, P], BF16, "Internal")
        self.kh_d = d("kh_s", [NCH, H, P, P], BF16, "Internal")
        self.vh_d = d("vh_s", [NCH, H, P, P], BF16, "Internal")
        self.x1_d = d("x1_s", [N, C], F32, "Internal")
        self.g_d = d("g_s", [HOT, P, N], F32, "Internal")
        self.w2q_d = d("w2q_s", [HID, C], BF16, "Internal")

    def alloc(self):
        sb = self.sb
        N, NCH = self.N, self.NCH
        NW = self.NWIN
        self.nwrep = sb("nwrep", [P, C], F32)
        self.repA = sb("repA", [P, C], F32)
        self.idb = sb("idb", [P, P], BF16)
        self.idf = sb("idf", [P, P], F32)
        self.mJ = sb("mJ", [1, 256], BF16)
        self.mB = sb("mB", [8, 256], BF16)
        self.onesc = sb("onesc", [P, 1], F32)
        self.onesrow = sb("onesrow", [1, P], F32)
        self.cccol = sb("cccol", [P, 1], F32)
        for nm in ("am1T", "sc1T", "al1T", "am2T", "rs2T", "al2T",
                   "am4T", "al4T"):
            setattr(self, nm, sb(nm, [P, NCH], F32))
        for nm in ("wsqkv", "wsgate", "wsproj", "wsw1", "wsw2", "thw1", "thw2"):
            setattr(self, nm, sb(nm, [P, 1], F32))
        self.wmp = sb("wmp", [P, 8], F32)
        self.tokrow = sb("tokrow", [1, N], F32)
        self.biascol = sb("biascol", [P, 64], F32)
        self.pbcol = sb("pbcol", [P, 16], F32)
        self.decay = sb("decay", [P, H * NCH], F32)
        self.Ssb = sb("Ssb", [P, H * D], F32)
        self.Sbb = sb("Sbb", [P, H * D], BF16)
        self.t0 = sb("t0", [P, C], F32)
        self.t1 = sb("t1", [P, C], F32)
        for nm in ("s_a", "s_b", "s_c", "s_d", "s_e", "s_f"):
            setattr(self, nm, sb(nm, [P, 1], F32))
        self.bsv = sb("bsv", [P, 1], F32)
        self.bss = sb("bss", [P, 1], F32)
        self.bsg = sb("bsg", [P, 1], F32)
        self.dma_pool = [self.sem(f"dma{i}") for i in range(16)]
        self.dma_rr = 0
        self.gp_pool = [self.sem(f"gdma{i}") for i in range(8)]
        self.gp_rr = 0
        self.sTE = self.sem("sTE")
        self.sVE = self.sem("sVE")
        self.sSE = self.sem("sSE")
        self.sGE = self.sem("sGE")
        self.sc_big0 = ExitStack()
        self.big0 = sb("big0", [P, 16 * C], BF16, scope=self.sc_big0)
        self.block = self.stk.enter_context(self.nc.Block())

    def cols_to_row(self, colsT, row_out, pre_ev=None, scale=None, scope=None):
        te_evs, ve_evs = [], []
        self._c2r_n = getattr(self, "_c2r_n", 0) + 1
        pps = self.ps(f"pc2r{self._c2r_n}_{row_out.name}", [1, P], F32, scope=scope)
        for ci in range(self.NCH):
            @self.block.tensor
            def _(t, ci=ci):
                self.wt(t, pre_ev if ci == 0 else ve_evs[ci - 1])
                ins = t.transpose(pps[:], colsT[:, ci:ci + 1], self.idf[:])
                te_evs.append(self.inc(self.sTE, ins))

            @self.block.vector
            def _(v, ci=ci):
                self.wt(v, te_evs[ci])
                c0 = ci * P
                if scale is None:
                    ins = v.tensor_copy(row_out[0:1, c0:c0 + P], pps[:])
                else:
                    ins = v.tensor_scalar(row_out[0:1, c0:c0 + P], pps[:], scale, None, ALU.mult)
                ve_evs.append(self.inc(self.sVE, ins))
        return ve_evs[-1]

    def replicate_rows(self, rows, rep_out, pre_ev=None, scope=None):
        te_evs, ve_evs = [], []
        WN = self.WN
        self._rr_n = getattr(self, "_rr_n", 0) + 1
        pps = self.ps(f"prep{self._rr_n}_{rep_out.name}", [P, WN], F32, scope=scope)
        for w in range(self.NWIN):
            @self.block.tensor
            def _(t, w=w):
                self.wt(t, pre_ev if w == 0 else ve_evs[w - 1])
                ins = t.matmul(pps[:], self.onesrow[:], rows[0:1, WN * w:WN * (w + 1)], start=True, stop=True)
                te_evs.append(self.inc(self.sTE, ins))

            @self.block.vector
            def _(v, w=w):
                self.wt(v, te_evs[w])
                ve_evs.append(self.inc(self.sVE, v.tensor_copy(rep_out[:, WN * w:WN * (w + 1)], pps[:])))
        return ve_evs[-1]

    def quant_round(self, v, src_ap, scale_col, out_ap, tmp_a, tmp_b):
        e = self.inc(self.sVE, v.tensor_scalar(tmp_a, src_ap, scale_col, M23, ALU.mult, ALU.add))
        self.wt(v, e)
        e = self.inc(self.sVE, v.tensor_scalar(tmp_a, tmp_a, M23, None, ALU.subtract))
        self.wt(v, e)
        e = self.inc(self.sVE, v.tensor_scalar(tmp_b, tmp_a, M52, M52, ALU.add, ALU.subtract))
        self.wt(v, e)
        e = self.inc(self.sVE, v.scalar_tensor_tensor(out_ap, tmp_a, self.cccol[:, 0:1], tmp_b, ALU.mult, ALU.add))
        self.wt(v, e)
        return e

    def phase_consts(self):
        b = self.block

        @b.sync
        def _(sy):
            g1 = self.dgrp(sy)
            for dst, src in [(self.idb[:], self.idb_d), (self.idf[:], self.idf_d),
                             (self.mJ[:], self.mJ_d), (self.mB[:], self.mB_d)]:
                self.inc(g1, sy.dma_start(out=dst, in_=src), 16)
            self.ev_cload = self.inc(g1, sy.dma_start(out=self.t1[0:1, 0:C], in_=self.nwrow_d), 16)
            self.idf_ready = self.ev_cload

        brows = self.sb("brows", [64, P], F32, scope=self.sc_big0)

        @b.gpsimd
        def _(g):
            gb = self.dgrp(g)
            self.inc(gb, g.dma_start(out=brows[0:48, :],
                     in_=self.qkvb_d.rearrange("o (t p) -> (o t) p", p=P)), 16)
            self.ev_bload = self.inc(gb, g.dma_start(out=brows[48:64, :],
                     in_=self.gateb_d.rearrange("o (t p) -> (o t) p", p=P)), 16)

        @b.vector
        def _(v):
            self.wt(v, self.ev_cload, self.ev_bload)
            v.memset(self.onesc[:], 1.0)
            v.memset(self.onesrow[:], 1.0)
            v.memset(self.Ssb[:], 0.0)
            v.memset(self.Sbb[:], 0.0)
            e = self.inc(self.sVE, v.memset(self.cccol[:], 0.0))
            self.wt(v, e)
            e = self.inc(self.sVE, v.tensor_scalar(self.s_a[:], self.onesc[:], 2.0 ** -30, None, ALU.add))
            self.wt(v, e)
            e = self.inc(self.sVE, v.tensor_scalar(self.s_a[:], self.s_a[:], 1.0, 2.0 ** 30, ALU.subtract, ALU.mult))
            self.wt(v, e)
            e = self.inc(self.sVE, v.tensor_scalar(self.cccol[:], self.s_a[:], -1.0, 1.0, ALU.mult, ALU.add))
            self.wt(v, e)
            self.ev_bias = e

        sc_c = ExitStack()
        pbt = self.ps("p_bt", [P, 64], F32, scope=sc_c)

        @b.tensor
        def _(t):
            self.wt(t, self.ev_bload, self.ev_bias)
            ins = t.transpose(pbt[:, 0:64], brows[0:64, :], self.idf[0:64, 0:64])
            self.ev_btr = self.inc(self.sTE, ins)

        @b.vector
        def _(v):
            self.wt(v, self.ev_btr)
            e = self.inc(self.sVE, v.tensor_copy(self.biascol[:], pbt[:, 0:64]))
            self.wt(v, e)
            self.ev_bcol = e

        @b.gpsimd
        def _(g):
            self.wt(g, self.ev_bcol)
            gb = self.dgrp(g)
            self.ev_pbload = self.inc(gb, g.dma_start(out=brows[0:16, :],
                     in_=self.projb_d.rearrange("o (t p) -> (o t) p", p=P)), 16)

        @b.tensor
        def _(t):
            self.wt(t, self.ev_pbload)
            ins = t.transpose(pbt[:, 0:16], brows[0:16, :], self.idf[0:16, 0:16])
            self.ev_btr2 = self.inc(self.sTE, ins)

        @b.vector
        def _(v):
            self.wt(v, self.ev_btr2)
            e = self.inc(self.sVE, v.tensor_copy(self.pbcol[:], pbt[:, 0:16]))
            self.wt(v, e)
            self.ev_pbcol = e

        e = self.replicate_rows_from_row(self.t1[0:1, 0:C], self.nwrep, pre_ev=self.ev_bias, scope=sc_c)
        self.consts_done = e
        sc_c.close()

    def replicate_rows_from_row(self, row_ap, rep_out, pre_ev=None, scope=None, src_cols=None):
        te_evs, ve_evs = [], []
        if src_cols is not None:
            rtmp = self.sb(f"rr_{rep_out.name}", [1, C], F32, scope=scope)
            pps0 = self.ps(f"prr0_{rep_out.name}", [1, P], F32, scope=scope)
            tev, vev = [], []
            for ci in range(16):
                @self.block.tensor
                def _(t, ci=ci):
                    if ci == 0:
                        self.wt(t, pre_ev, self.ev_pbcol)
                    else:
                        self.wt(t, vev[ci - 1])
                    ins = t.transpose(pps0[:], src_cols[:, ci:ci + 1], self.idf[:])
                    tev.append(self.inc(self.sTE, ins))

                @self.block.vector
                def _(v, ci=ci):
                    self.wt(v, tev[ci])
                    vev.append(self.inc(self.sVE, v.tensor_copy(rtmp[0:1, ci * P:(ci + 1) * P], pps0[:])))
            row_ap = rtmp
            pre_ev = vev[-1]
        self._rrr_n = getattr(self, "_rrr_n", 0) + 1
        pps = self.ps(f"prr{self._rrr_n}_{rep_out.name}", [P, 512], F32, scope=scope)
        for w in range(4):
            @self.block.tensor
            def _(t, w=w):
                self.wt(t, pre_ev if w == 0 else ve_evs[w - 1])
                ins = t.matmul(pps[:], self.onesrow[:], row_ap[0:1, 512 * w:512 * (w + 1)], start=True, stop=True)
                te_evs.append(self.inc(self.sTE, ins))

            @self.block.vector
            def _(v, w=w):
                self.wt(v, te_evs[w])
                ve_evs.append(self.inc(self.sVE, v.tensor_copy(rep_out[:, 512 * w:512 * (w + 1)], pps[:])))
        return ve_evs[-1]

    def phase_wmeans(self):
        b = self.block
        sc_w = ExitStack()
        wbuf = [self.sb(f"wmb{i}", [P, 2048], F32, scope=sc_w) for i in range(2)]
        acc = self.sb("wm_acc", [P, 1], F32, scope=sc_w)
        st = dict(gp=[], se=[], ve=[])
        mats = [(self.qkvT_d, C, 3 * C, 0), (self.gateT_d, C, C, 1),
                (self.projT_d, C, C, 2), (self.w1T_d, C, 8192, 3),
                (self.w2T_d, 8192, C, 4)]
        if self.PHASES < 1:
            mats = mats[:3]
        first_for = set()
        for src, nr, ncol, pi in mats:
            for it in range(nr // P):
                for hf in range(ncol // 2048):
                    k = len(st["gp"])
                    slot = k % 2

                    @b.gpsimd
                    def _(g, src=src, it=it, hf=hf, slot=slot, k=k):
                        if k >= 2:
                            self.wt(g, st["se"][k - 2])
                        gb = self.dgrp(g)
                        st["gp"].append(self.inc(gb, g.dma_start(
                            out=wbuf[slot][:], in_=src[it * P:(it + 1) * P, hf * 2048:(hf + 1) * 2048]), 16))

                    @b.scalar
                    def _(sc, slot=slot, k=k):
                        self.wt(sc, st["gp"][k], self.consts_done)
                        if st["ve"]:
                            self.wt(sc, st["ve"][-1])
                        st["se"].append(self.inc(self.sSE, sc.activation(
                            self.t1[:], wbuf[slot][:], ACT.Abs, accum_out=acc[:])))

                    @b.vector
                    def _(v, pi=pi, k=k):
                        self.wt(v, st["se"][k])
                        if pi not in first_for:
                            first_for.add(pi)
                            st["ve"].append(self.inc(self.sVE, v.tensor_copy(self.wmp[:, pi:pi + 1], acc[:])))
                        else:
                            st["ve"].append(self.inc(self.sVE, v.tensor_tensor(
                                self.wmp[:, pi:pi + 1], self.wmp[:, pi:pi + 1], acc[:], ALU.add)))

        pws = self.ps("p_ws", [1, 8], F32, scope=sc_w)
        wsrow = self.sb("wsrow", [1, 8], F32, scope=sc_w)

        @b.tensor
        def _(t):
            self.wt(t, st["ve"][-1])
            ins = t.matmul(pws[0:1, 0:5], self.onesc[:], self.wmp[:, 0:5], start=True, stop=True)
            self.ev_wsum = self.inc(self.sTE, ins)

        @b.vector
        def _(v):
            self.wt(v, self.ev_wsum)
            nel = [3 * C * C, C * C, C * C, C * 8192, 8192 * C]
            last = None
            for i in range(5):
                last = self.inc(self.sVE, v.tensor_scalar(
                    wsrow[0:1, i:i + 1], pws[0:1, i:i + 1], 1.0 / nel[i], 1e-5, ALU.mult, ALU.max))
            self.wt(v, last)
            self.ev_wsrow = last

        pbc = self.ps("p_bc", [P, 8], F32, scope=sc_w)

        @b.tensor
        def _(t):
            self.wt(t, self.ev_wsrow)
            ins = t.matmul(pbc[:, 0:5], self.onesrow[:], wsrow[0:1, 0:5], start=True, stop=True)
            self.ev_wsbc = self.inc(self.sTE, ins)

        @b.vector
        def _(v):
            self.wt(v, self.ev_wsbc)
            e = None
            for i, dst in enumerate([self.wsqkv, self.wsgate, self.wsproj, self.wsw1, self.wsw2]):
                e = self.inc(self.sVE, v.tensor_copy(dst[:], pbc[:, i:i + 1]))
            self.wt(v, e)
            self.ev_ws = e

        @b.vector
        def _(v):
            self.wt(v, self.ev_ws)
            self.inc(self.sVE, v.tensor_scalar(self.thw1[:], self.wsw1[:], 0.5, None, ALU.mult))
            e = self.inc(self.sVE, v.tensor_scalar(self.thw2[:], self.wsw2[:], 0.5, None, ALU.mult))
            self.wt(v, e)
            self.ev_th = e
        self.sc_w = sc_w

    def phase_p0(self):
        b = self.block
        NCH = self.NCH
        self.sc_p0 = ExitStack()
        xch = [self.sb(f"xch{i}", [P, C], F32, scope=self.sc_p0) for i in range(2)]
        xq_tm = self.sb("xq_tm", [P, C], BF16, scope=self.sc_p0)
        ptr = self.ps("p_tr0", [P, C], BF16, scope=self.sc_p0)
        self.p0_ld, self.p0_ve, self.p0_sq, self.p0_sqrt, self.p0_tr = [], [], [], [], []
        self.p0_fm = []

        for ci in range(NCH):
            slot = ci % 2

            @b.sync
            def _(sy, ci=ci, slot=slot):
                if ci >= 2:
                    self.wt(sy, self.p0_ve[ci - 2]["xn"])
                gb = self.dgrp(sy)
                self.p0_ld.append(self.inc(gb, sy.dma_start(
                    out=xch[slot][:], in_=self.x_d[ci * P:(ci + 1) * P, :]), 16))

            @b.scalar
            def _(sc, ci=ci, slot=slot):
                self.wt(sc, self.p0_ld[ci], self.ev_th)
                if ci >= 1:
                    self.wt(sc, self.p0_ve[ci - 1]["xq"])
                self.p0_sq.append(self.inc(self.sSE, sc.activation(
                    self.t0[:], xch[slot][:], ACT.Square, accum_out=self.s_b[:])))

            @b.vector
            def _(v, ci=ci, slot=slot):
                evs = {}
                self.wt(v, self.p0_ld[ci], self.consts_done)
                e = self.inc(self.sVE, v.reduce_sum(self.s_a[:], xch[slot][:], AX))
                self.wt(v, e, self.p0_sq[ci])
                e = self.inc(self.sVE, v.tensor_scalar(self.s_a[:], self.s_a[:], 1.0 / C, None, ALU.mult))
                self.wt(v, e)
                e = self.inc(self.sVE, v.tensor_tensor(self.s_c[:], self.s_a[:], self.s_a[:], ALU.mult))
                self.wt(v, e)
                e = self.inc(self.sVE, v.scalar_tensor_tensor(self.s_c[:], self.s_b[:], 1.0 / C, self.s_c[:], ALU.mult, ALU.subtract))
                self.wt(v, e)
                e = self.inc(self.sVE, v.tensor_scalar(self.s_c[:], self.s_c[:], EPS1, None, ALU.add))
                evs["var"] = e
                self.p0_ve.append(evs)

            @b.scalar
            def _(sc, ci=ci):
                self.wt(sc, self.p0_ve[ci]["var"])
                self.p0_sqrt.append(self.inc(self.sSE, sc.activation(self.s_d[:], self.s_c[:], ACT.Sqrt)))

            @b.vector
            def _(v, ci=ci, slot=slot):
                evs = self.p0_ve[ci]
                self.wt(v, self.p0_sqrt[ci])
                e = self.inc(self.sVE, v.reciprocal(self.s_d[:], self.s_d[:]))
                self.wt(v, e)
                e = self.inc(self.sVE, v.tensor_scalar(self.t1[:], xch[slot][:], self.s_a[:, 0:1], self.s_d[:, 0:1], ALU.subtract, ALU.mult))
                evs["xn"] = e
                self.wt(v, e)
                e = self.inc(self.sVE, v.reduce_max(self.s_e[:], self.t1[:], AX, apply_absolute_value=True))
                self.wt(v, e)
                e = self.inc(self.sVE, v.tensor_scalar(self.s_e[:], self.s_e[:], 1e-5, None, ALU.max))
                self.wt(v, e)
                self.inc(self.sVE, v.tensor_copy(self.am1T[:, ci:ci + 1], self.s_e[:]))
                self.inc(self.sVE, v.tensor_scalar(self.al1T[:, ci:ci + 1], self.s_e[:], 1.0 / 127.0, None, ALU.mult))
                e = self.inc(self.sVE, v.reciprocal(self.s_e[:], self.s_e[:]))
                self.wt(v, e)
                e = self.inc(self.sVE, v.tensor_scalar(self.s_e[:], self.s_e[:], 127.0, None, ALU.mult))
                self.wt(v, e)
                e = self.inc(self.sVE, v.tensor_copy(self.sc1T[:, ci:ci + 1], self.s_e[:]))
                self.wt(v, e)
                if ci >= 1:
                    self.wt(v, self.p0_tr[ci - 1])
                    self.wt(v, self.p0_ve[ci - 1].get("dbgst"))
                e = self.inc(self.sVE, v.tensor_scalar(self.t0[:], self.t1[:], self.s_e[:, 0:1], M23, ALU.mult, ALU.add))
                self.wt(v, e)
                e = self.inc(self.sVE, v.tensor_scalar(self.t0[:], self.t0[:], M23, None, ALU.subtract))
                self.wt(v, e)
                e = self.inc(self.sVE, v.tensor_scalar(self.t1[:], self.t0[:], M52, M52, ALU.add, ALU.subtract))
                self.wt(v, e)
                e = self.inc(self.sVE, v.scalar_tensor_tensor(xq_tm[:], self.t0[:], self.cccol[:, 0:1], self.t1[:], ALU.mult, ALU.add))
                self.wt(v, e)
                evs["xq"] = e
                evs["t0free"] = e

            @b.tensor
            def _(t, ci=ci):
                self.wt(t, self.p0_ve[ci]["xq"], self.consts_done)
                if ci >= 1:
                    self.wt(t, self.p0_fm[ci - 1])
                ins = None
                for idx in range(16):
                    ins = t.transpose(ptr[:, P * idx:P * (idx + 1)], xq_tm[:, P * idx:P * (idx + 1)], self.idb[:])
                self.p0_tr.append(self.inc(self.sTE, ins))

            @b.vector
            def _(v, ci=ci):
                self.wt(v, self.p0_tr[ci])
                e = None
                for idx in range(16):
                    e = self.inc(self.sVE, v.tensor_copy(
                        self.big0[:, idx * C + ci * P: idx * C + (ci + 1) * P],
                        ptr[:, P * idx:P * (idx + 1)]))
                self.p0_fm.append(e)
                self.p0_ve[ci]["fm"] = e

            if self.DBG:
                o = self.dbg_out("xq", [self.N, C])

                @b.gpsimd
                def _(g, ci=ci):
                    self.wt(g, self.p0_ve[ci]["xq"])
                    gb = self.dgrp(g)
                    self.p0_ve[ci]["dbgst"] = self.inc(gb, g.dma_start(out=o[ci * P:(ci + 1) * P, :], in_=xq_tm[:]), 16)

        e = self.cols_to_row(self.al1T, self.tokrow, pre_ev=self.p0_ve[-1]["xq"], scope=self.sc_p0)
        e = self.replicate_rows(self.tokrow, self.repA, pre_ev=e, scope=self.sc_p0)
        self.p0_done = e
        self.sc_p0.close()

    def phase_p1(self):
        b = self.block
        N, NCH = self.N, self.NCH
        self.sc_p1 = ExitStack()
        wf = [self.sb(f"p1wf{i}", [P, 16 * P], F32, scope=self.sc_p1) for i in range(2)]
        wq = [self.sb(f"p1wq{i}", [P, 16 * P], BF16, scope=self.sc_p1) for i in range(2)]
        evq = [self.sb(f"p1evq{i}", [P, C], BF16, scope=self.sc_p1) for i in range(2)]
        evgs = self.sb("p1evg", [P, C], F32, scope=self.sc_p1)
        pmm = [self.ps(f"p1mm{i}", [P, C], F32, scope=self.sc_p1) for i in range(2)]
        st = dict(wld=[], wq=[], mm=[], ev=[], stq=[], sig=[], dec=[])

        otiles = [("qkv", i) for i in range(48)] + [("gate", i) for i in range(16)]
        for oi, (kind, ot) in enumerate(otiles):
            slot = oi % 2
            src = self.qkvT_d if kind == "qkv" else self.gateT_d
            cb = ot * P

            @b.sync
            def _(sy, oi=oi, slot=slot, src=src, cb=cb):
                if oi >= 2:
                    self.wt(sy, st["wq"][oi - 2])
                gb = self.dgrp(sy)
                e = None
                for it in range(16):
                    e = self.inc(gb, sy.dma_start(
                        out=wf[slot][:, it * P:(it + 1) * P],
                        in_=src[it * P:(it + 1) * P, cb:cb + P]), 16)
                st["wld"].append(e)

            @b.scalar
            def _(sc, oi=oi, slot=slot):
                self.wt(sc, st["wld"][oi])
                if oi >= 2:
                    self.wt(sc, st["mm"][oi - 2])
                st["wq"].append(self.inc(self.sSE, sc.activation(wq[slot][:], wf[slot][:], ACT.Sign)))

            @b.tensor
            def _(t, oi=oi, slot=slot, kind=kind, ot=ot):
                self.wt(t, st["wq"][oi], self.p0_done, self.ev_bcol)
                if oi >= 2:
                    self.wt(t, st["ev"][oi - 2])
                pm = pmm[slot]
                WN = self.WN
                ins = None
                for it in range(16):
                    for w in range(self.NWIN):
                        ins = t.matmul(pm[:, WN * w:WN * (w + 1)],
                                 wq[slot][:, it * P:(it + 1) * P],
                                 self.big0[:, it * C + WN * w: it * C + WN * (w + 1)],
                                 start=(it == 0), stop=(it == 15))
                st["mm"].append(self.inc(self.sTE, ins))

            if kind == "qkv":
                @b.vector
                def _(v, oi=oi, slot=slot):
                    self.wt(v, st["mm"][oi])
                    if len(st["stq"]) >= 2:
                        self.wt(v, st["stq"][-2])
                    e = self.inc(self.sVE, v.tensor_tensor(self.t0[:, 0:N], pmm[slot][:, 0:N], self.repA[:, 0:N], ALU.mult))
                    self.wt(v, e)
                    e = self.inc(self.sVE, v.tensor_scalar(evq[slot][:, 0:N], self.t0[:, 0:N],
                                                           self.wsqkv[:, 0:1], self.biascol[:, oi:oi + 1], ALU.mult, ALU.add))
                    self.wt(v, e)
                    st["ev"].append(e)

                @b.scalar
                def _(sc, oi=oi, ot=ot, slot=slot):
                    self.wt(sc, st["ev"][oi])
                    kq, h = ot // H, ot % H
                    dst = [self.qh_d, self.kh_d, self.vh_d][kq]
                    gb = self.dgrp(sc)
                    e = None
                    for ci in range(NCH):
                        e = self.inc(gb, sc.dma_start(out=dst[ci, h], in_=evq[slot][:, ci * P:(ci + 1) * P]), 16)
                    st["stq"].append(e)
            else:
                @b.vector
                def _(v, oi=oi, slot=slot):
                    self.wt(v, st["mm"][oi])
                    if st["sig"]:
                        self.wt(v, st["sig"][-1])
                    e = self.inc(self.sVE, v.tensor_tensor(self.t0[:, 0:N], pmm[slot][:, 0:N], self.repA[:, 0:N], ALU.mult))
                    self.wt(v, e)
                    e = self.inc(self.sVE, v.tensor_scalar(evgs[:, 0:N], self.t0[:, 0:N],
                                                           self.wsgate[:, 0:1], self.biascol[:, oi:oi + 1], ALU.mult, ALU.add))
                    self.wt(v, e)
                    st["ev"].append(e)

                @b.scalar
                def _(sc, oi=oi):
                    self.wt(sc, st["ev"][oi])
                    st["sig"].append(self.inc(self.sSE, sc.activation(evgs[:, 0:N], evgs[:, 0:N], ACT.Sigmoid)))

                @b.vector
                def _(v, oi=oi, ot=ot):
                    self.wt(v, st["sig"][-1])
                    e = self.inc(self.sVE, v.tensor_reduce(
                        self.t0[:, 0:NCH].rearrange("p (c o) -> p c o", o=1),
                        evgs[:, 0:N].rearrange("p (c l) -> p c l", c=NCH), AX, ALU.add))
                    self.wt(v, e)
                    e = self.inc(self.sVE, v.tensor_scalar(self.decay[:, ot * NCH:(ot + 1) * NCH],
                                                           self.t0[:, 0:NCH], 1.0 / 128.0, None, ALU.mult))
                    self.wt(v, e)
                    st["dec"].append(e)

            if self.DBG and kind == "qkv" and ot == 0:
                o = self.dbg_out("qkv", [P, self.N])

                @b.scalar
                def _(sc, oi=oi):
                    self.wt(sc, st["ev"][oi])
                    pass

                @b.vector
                def _(v, oi=oi, slot=slot):
                    self.wt(v, st["ev"][oi])
                    self.ev_dbgq = self.inc(self.sVE, v.tensor_copy(self.t1[:, 0:N], evq[slot][:, 0:N]))

                @b.scalar
                def _(sc, oi=oi):
                    self.wt(sc, self.ev_dbgq)
                    gb = self.dgrp(sc)
                    self.inc(gb, sc.dma_start(out=o[:, 0:self.N], in_=self.t1[:, 0:N]), 16)

        self.p1_st = st
        if self.DBG:
            o = self.dbg_out("decay", [P, H * NCH])

            @b.scalar
            def _(sc):
                self.wt(sc, st["dec"][-1])
                gb = self.dgrp(sc)
                self.inc(gb, sc.dma_start(out=o[:, 0:H * NCH], in_=self.decay[:, 0:H * NCH]), 16)
        self.sc_p1.close()

    def phase_p2(self):
        b = self.block
        NCH, N = self.NCH, self.N
        self.sc_p2 = ExitStack()
        sc2 = self.sc_p2
        qhb = [self.sb(f"qhb{i}", [P, H * P], BF16, scope=sc2) for i in range(2)]
        khb = [self.sb(f"khb{i}", [P, H * P], BF16, scope=sc2) for i in range(2)]
        vhb = [self.sb(f"vhb{i}", [P, H * P], BF16, scope=sc2) for i in range(2)]
        qmix = self.sb("qmix", [P, H * P], BF16, scope=sc2)
        kmix = self.sb("kmix", [P, H * P], BF16, scope=sc2)
        vmix = self.sb("vmix", [P, H * P], BF16, scope=sc2)
        probsN = self.sb("probsN", [P, H * P], BF16, scope=sc2)
        probsT = self.sb("probsT", [P, H * P], BF16, scope=sc2)
        vrows = self.sb("vrows", [P, H * P], BF16, scope=sc2)
        outmix = self.sb("outmix", [P, H * P], F32, scope=sc2)
        outT = self.sb("outT", [P, H * P], F32, scope=sc2)
        ktmw = self.sb("ktmw", [P, 8 * P], BF16, scope=sc2)
        vtmw = self.sb("vtmw", [P, 8 * P], BF16, scope=sc2)
        rmaxT = self.sb("rmaxT", [P, H], F32, scope=sc2)
        biasT = self.sb("biasT", [P, H], F32, scope=sc2)
        denomT = self.sb("denomT", [P, H], F32, scope=sc2)
        oq_tm = self.sb("oq_tm", [P, H * P], BF16, scope=sc2)
        psA = self.ps("psA", [P, 2048], F32, scope=sc2)
        psB = self.ps("psB", [P, 2048], BF16, scope=sc2)
        psE = self.ps("psE", [P, 1024], F32, scope=sc2)

        ld_evs, ev = {}, {}
        if self.DBG:
            o_out1 = self.dbg_out("out1", [N, C])
            o_oq = self.dbg_out("oq", [N, C])

        def emit_load(ci):
            if ci >= NCH:
                return

            @b.sync
            def _(sy, ci=ci):
                if ci >= 2:
                    self.wt(sy, ev[ci - 2]["recurmm1"], ev[ci - 2]["kvT1"])
                    self.wt(sy, ev[ci - 2]["qmix"], ev[ci - 2]["kvmix"])
                gb = self.dgrp(sy)
                slot = ci % 2
                e = None
                for buf, src in [(qhb[slot], self.qh_d), (khb[slot], self.kh_d), (vhb[slot], self.vh_d)]:
                    e = self.inc(gb, sy.dma_start(out=buf.rearrange("p (h l) -> p h l", h=H),
                                 in_=src[ci].rearrange("h d l -> d h l")), 16)
                ld_evs[ci] = e

        emit_load(0)
        emit_load(1)

        for ci in range(NCH):
            slot = ci % 2
            E = {}
            ev[ci] = E
            if ci + 2 < NCH:
                pass

            @b.vector
            def _(v, ci=ci, slot=slot):
                self.wt(v, ld_evs[ci])
                if ci >= 1:
                    self.wt(v, ev[ci - 1]["scoresmm"], ev[ci - 1]["vrowsT"])
                e = None
                for h in range(H):
                    e = self.inc(self.sVE, v.tensor_copy(
                        kmix.rearrange("p (l g) -> p g l", g=H)[:, h, :], khb[slot][:, h * P:(h + 1) * P]))
                for h in range(H):
                    e = self.inc(self.sVE, v.tensor_copy(
                        vmix.rearrange("p (l g) -> p g l", g=H)[:, h, :], vhb[slot][:, h * P:(h + 1) * P]))
                E["kvmix"] = e

            @b.scalar
            def _(sc, ci=ci, slot=slot):
                self.wt(sc, ld_evs[ci])
                if ci >= 1:
                    self.wt(sc, ev[ci - 1]["scoresmm"])
                e = None
                for h in range(H):
                    e = self.inc(self.sSE, sc.activation(
                        qmix.rearrange("p (l g) -> p g l", g=H)[:, h, :], qhb[slot][:, h * P:(h + 1) * P], ACT.Copy))
                E["qmix"] = e

            @b.tensor
            def _(t, ci=ci):
                self.wt(t, E["kvmix"], E["qmix"])
                if ci >= 1:
                    self.wt(t, ev[ci - 1]["J2"])
                ins = None
                for j in range(H):
                    sl = slice(j * P, (j + 1) * P)
                    t.matmul(psA[:, sl], self.mJ[0:1, 0:128], self.mJ[0:1, 128:256], start=True, stop=False)
                    t.matmul(psA[:, sl], self.mB[:, 0:128], self.mB[:, 128:256], start=False, stop=False)
                    ins = t.matmul(psA[:, sl], qmix[:, sl], kmix[:, sl], start=False, stop=True)
                E["scoresmm"] = self.inc(self.sTE, ins)

            @b.vector
            def _(v, ci=ci):
                self.wt(v, E["scoresmm"])
                if ci >= 1:
                    self.wt(v, ev[ci - 1]["expdone"])
                e = None
                for j in range(H):
                    e = self.inc(self.sVE, v.reduce_max(rmaxT[:, j:j + 1], psA[:, j * P:(j + 1) * P], AX))
                self.wt(v, e)
                e = self.inc(self.sVE, v.tensor_scalar(biasT[:], rmaxT[:], -SM_SCALE, None, ALU.mult))
                self.wt(v, e)
                E["bias"] = e

            @b.scalar
            def _(sc, ci=ci):
                self.wt(sc, E["bias"])
                if ci >= 1:
                    self.wt(sc, ev[ci - 1]["J3q"])
                    self.wt(sc, ev[ci - 1].get("dbg1"))
                e = None
                for j in range(H):
                    e = self.inc(self.sSE, sc.activation(
                        outT[:, j * P:(j + 1) * P], psA[:, j * P:(j + 1) * P], ACT.Exp,
                        bias=biasT[:, j:j + 1], scale=SM_SCALE, accum_out=denomT[:, j:j + 1]))
                E["expdone"] = e

            @b.vector
            def _(v, ci=ci):
                self.wt(v, E["expdone"])
                if ci >= 1:
                    self.wt(v, ev[ci - 1]["probsTT"])
                e = self.inc(self.sVE, v.reciprocal(denomT[:], denomT[:]))
                self.wt(v, e)
                e = None
                for j in range(H):
                    e = self.inc(self.sVE, v.tensor_scalar(
                        probsN[:, j * P:(j + 1) * P], outT[:, j * P:(j + 1) * P],
                        denomT[:, j:j + 1], None, ALU.mult))
                self.wt(v, e)
                E["probsN"] = e

            @b.tensor
            def _(t, ci=ci):
                self.wt(t, E["kvmix"])
                if ci >= 1:
                    self.wt(t, ev[ci - 1]["J5"])
                ins = None
                for j in range(H):
                    ins = t.transpose(psB[:, j * P:(j + 1) * P], vmix[:, j * P:(j + 1) * P], self.idb[:])
                E["vrowsT"] = self.inc(self.sTE, ins)

            @b.vector
            def _(v, ci=ci):
                self.wt(v, E["vrowsT"])
                if ci >= 1:
                    self.wt(v, ev[ci - 1]["localmm"])
                e = None
                for j in range(H):
                    e = self.inc(self.sVE, v.tensor_copy(vrows[:, j * P:(j + 1) * P], psB[:, j * P:(j + 1) * P]))
                E["vrows_evac"] = e

            @b.tensor
            def _(t, ci=ci):
                self.wt(t, E["probsN"], E["vrows_evac"])
                ins = None
                for j in range(H):
                    ins = t.transpose(psB[:, j * P:(j + 1) * P], probsN[:, j * P:(j + 1) * P], self.idb[:])
                E["probsTT"] = self.inc(self.sTE, ins)

            @b.scalar
            def _(sc, ci=ci):
                self.wt(sc, E["probsTT"])
                if ci >= 1:
                    self.wt(sc, ev[ci - 1]["localmm"])
                e = None
                for j in range(H):
                    e = self.inc(self.sSE, sc.activation(probsT[:, j * P:(j + 1) * P],
                                                         psB[:, j * P:(j + 1) * P], ACT.Copy))
                E["probsT_evac"] = e

            @b.tensor
            def _(t, ci=ci):
                self.wt(t, E["probsT_evac"], E["expdone"])
                ins = None
                for j in range(H):
                    ins = t.matmul(psA[:, j * P:(j + 1) * P], vrows[:, j * P:(j + 1) * P],
                                   probsT[:, j * P:(j + 1) * P], start=True, stop=True)
                E["localmm"] = self.inc(self.sTE, ins)

            @b.vector
            def _(v, ci=ci):
                self.wt(v, E["localmm"], E["probsN"])
                e = self.inc(self.sVE, v.tensor_copy(outmix[:], psA[:]))
                self.wt(v, e)
                E["outmix_evac"] = e

            if ci >= 1:
                for half in range(2):
                    @b.tensor
                    def _(t, ci=ci, half=half, slot=slot):
                        if half == 0:
                            self.wt(t, E["outmix_evac"])
                            if ci >= 2:
                                self.wt(t, ev[ci - 1]["Supd1"])
                            else:
                                self.wt(t, ev[ci - 1].get("Supd1"))
                        else:
                            self.wt(t, E["recur_add0"])
                        ins = None
                        for hh in range(8):
                            h = half * 8 + hh
                            ins = t.matmul(psE[:, hh * P:(hh + 1) * P],
                                           self.Sbb[:, h * P:(h + 1) * P],
                                           qhb[slot][:, h * P:(h + 1) * P], start=True, stop=True)
                        E[f"recurmm{half}"] = self.inc(self.sTE, ins)

                    @b.vector
                    def _(v, ci=ci, half=half):
                        self.wt(v, E[f"recurmm{half}"])
                        e = None
                        for hh in range(8):
                            h = half * 8 + hh
                            dst = outmix.rearrange("p (l g) -> p g l", g=H)[:, h, :]
                            e = self.inc(self.sVE, v.tensor_tensor(dst, dst, psE[:, hh * P:(hh + 1) * P], ALU.add))
                        E[f"recur_add{half}"] = e
            else:
                E["recurmm1"] = E["outmix_evac"]
                E["recur_add0"] = E["outmix_evac"]
                E["recur_add1"] = E["outmix_evac"]

            for half in range(2):
                @b.tensor
                def _(t, ci=ci, half=half, slot=slot):
                    self.wt(t, E["probsT_evac"])
                    if half == 1:
                        self.wt(t, E["kvtm_evac0"])
                    ins = None
                    for hh in range(8):
                        h = half * 8 + hh
                        t.transpose(psB[:, (2 * hh) * P:(2 * hh + 1) * P], khb[slot][:, h * P:(h + 1) * P], self.idb[:])
                        ins = t.transpose(psB[:, (2 * hh + 1) * P:(2 * hh + 2) * P], vhb[slot][:, h * P:(h + 1) * P], self.idb[:])
                    E[f"kvT{half}"] = self.inc(self.sTE, ins)

                @b.scalar
                def _(sc, ci=ci, half=half):
                    self.wt(sc, E[f"kvT{half}"])
                    if half == 1:
                        self.wt(sc, E["Umm0"])
                    e = None
                    for hh in range(8):
                        e = self.inc(self.sSE, sc.activation(ktmw[:, hh * P:(hh + 1) * P],
                                                             psB[:, (2 * hh) * P:(2 * hh + 1) * P], ACT.Copy))
                        e = self.inc(self.sSE, sc.activation(vtmw[:, hh * P:(hh + 1) * P],
                                                             psB[:, (2 * hh + 1) * P:(2 * hh + 2) * P], ACT.Copy))
                    E[f"kvtm_evac{half}"] = e

                @b.tensor
                def _(t, ci=ci, half=half):
                    self.wt(t, E[f"kvtm_evac{half}"], E["recur_add1"])
                    if half == 1:
                        self.wt(t, E["Supd0"])
                    ins = None
                    for hh in range(8):
                        ins = t.matmul(psE[:, hh * P:(hh + 1) * P], ktmw[:, hh * P:(hh + 1) * P],
                                       vtmw[:, hh * P:(hh + 1) * P], start=True, stop=True)
                    E[f"Umm{half}"] = self.inc(self.sTE, ins)

                @b.vector
                def _(v, ci=ci, half=half):
                    self.wt(v, E[f"Umm{half}"])
                    e = None
                    for hh in range(8):
                        h = half * 8 + hh
                        e = self.inc(self.sVE, v.tensor_scalar(
                            self.Ssb[:, h * P:(h + 1) * P], self.Ssb[:, h * P:(h + 1) * P],
                            self.decay[:, h * NCH + ci:h * NCH + ci + 1], None, ALU.mult))
                    self.wt(v, e)
                    e2 = None
                    for hh in range(8):
                        h = half * 8 + hh
                        e2 = self.inc(self.sVE, v.tensor_tensor(
                            self.Ssb[:, h * P:(h + 1) * P], self.Ssb[:, h * P:(h + 1) * P],
                            psE[:, hh * P:(hh + 1) * P], ALU.add))
                    self.wt(v, e2)
                    e3 = None
                    for hh in range(8):
                        h = half * 8 + hh
                        e3 = self.inc(self.sVE, v.tensor_copy(
                            self.Sbb[:, h * P:(h + 1) * P], self.Ssb[:, h * P:(h + 1) * P]))
                    self.wt(v, e3)
                    E[f"Supd{half}"] = e3

            @b.tensor
            def _(t, ci=ci):
                self.wt(t, E["recur_add1"])
                ins = None
                for h in range(H):
                    src = outmix.rearrange("p (l g) -> p g l", g=H)[:, h, :]
                    ins = t.transpose(psA[:, h * P:(h + 1) * P], src, self.idf[:])
                E["J1"] = self.inc(self.sTE, ins)

            @b.vector
            def _(v, ci=ci):
                self.wt(v, E["J1"])
                e = None
                for h in range(H):
                    e = self.inc(self.sVE, v.tensor_copy(outT[:, h * P:(h + 1) * P], psA[:, h * P:(h + 1) * P]))
                E["J2"] = e

            @b.scalar
            def _(sc, ci=ci):
                self.wt(sc, E["J2"])
                if ci >= 1:
                    self.wt(sc, ev[ci - 1]["J3q"])
                E["Jssq"] = self.inc(self.sSE, sc.activation(self.t0[:], outT[:], ACT.Square, accum_out=self.s_b[:]))

            @b.vector
            def _(v, ci=ci):
                self.wt(v, E["J2"], E["Jssq"])
                e = self.inc(self.sVE, v.tensor_tensor(outmix[:], outT[:], self.nwrep[:], ALU.mult))
                self.wt(v, e)
                e = self.inc(self.sVE, v.reduce_max(self.s_e[:], outmix[:], AX, apply_absolute_value=True))
                self.wt(v, e)
                e = self.inc(self.sVE, v.tensor_scalar(self.s_e[:], self.s_e[:], 1e-5, None, ALU.max))
                self.wt(v, e)
                e = self.inc(self.sVE, v.tensor_copy(self.am2T[:, ci:ci + 1], self.s_e[:]))
                e = self.inc(self.sVE, v.tensor_scalar(self.s_c[:], self.s_b[:], 1.0 / C, EPS2, ALU.mult, ALU.add))
                self.wt(v, e)
                E["J3a"] = e

            @b.scalar
            def _(sc, ci=ci):
                self.wt(sc, E["J3a"])
                E["J3sqrt"] = self.inc(self.sSE, sc.activation(self.s_d[:], self.s_c[:], ACT.Sqrt))

            @b.vector
            def _(v, ci=ci):
                self.wt(v, E["J3sqrt"])
                e = self.inc(self.sVE, v.reciprocal(self.s_d[:], self.s_d[:]))
                self.wt(v, e)
                e = self.inc(self.sVE, v.tensor_copy(self.rs2T[:, ci:ci + 1], self.s_d[:]))
                e = self.inc(self.sVE, v.tensor_tensor(self.s_a[:], self.s_e[:], self.s_d[:], ALU.mult))
                self.wt(v, e)
                e = self.inc(self.sVE, v.tensor_scalar(self.s_a[:], self.s_a[:], self.wsproj[:, 0:1], 1.0 / 127.0, ALU.mult, ALU.mult))
                self.wt(v, e)
                e = self.inc(self.sVE, v.tensor_copy(self.al2T[:, ci:ci + 1], self.s_a[:]))
                e = self.inc(self.sVE, v.reciprocal(self.s_e[:], self.s_e[:]))
                self.wt(v, e)
                e = self.inc(self.sVE, v.tensor_scalar(self.s_e[:], self.s_e[:], 127.0, None, ALU.mult))
                self.wt(v, e)
                e = self.inc(self.sVE, v.tensor_scalar(self.t0[:], outmix[:], self.s_e[:, 0:1], M23, ALU.mult, ALU.add))
                self.wt(v, e)
                e = self.inc(self.sVE, v.tensor_scalar(self.t0[:], self.t0[:], M23, None, ALU.subtract))
                self.wt(v, e)
                e = self.inc(self.sVE, v.tensor_scalar(self.t1[:], self.t0[:], M52, M52, ALU.add, ALU.subtract))
                self.wt(v, e)
                if ci >= 1:
                    self.wt(v, ev[ci - 1]["J4"], ev[ci - 1].get("dbg1"))
                e = self.inc(self.sVE, v.scalar_tensor_tensor(oq_tm[:], self.t0[:], self.cccol[:, 0:1], self.t1[:], ALU.mult, ALU.add))
                self.wt(v, e)
                E["J3q"] = e

            @b.tensor
            def _(t, ci=ci):
                self.wt(t, E["J3q"], E["kvtm_evac1"])
                if ci >= 1:
                    pass
                ins = None
                for h in range(H):
                    ins = t.transpose(psB[:, h * P:(h + 1) * P], oq_tm[:, h * P:(h + 1) * P], self.idb[:])
                E["J4"] = self.inc(self.sTE, ins)

            @b.vector
            def _(v, ci=ci):
                self.wt(v, E["J4"])
                e = None
                for h in range(H):
                    e = self.inc(self.sVE, v.tensor_copy(
                        self.big0[:, h * C + ci * P: h * C + (ci + 1) * P],
                        psB[:, h * P:(h + 1) * P]))
                E["J5"] = e

            if self.DBG:
                @b.gpsimd
                def _(g, ci=ci):
                    self.wt(g, E["J3q"], E["Supd1"])
                    gb = self.dgrp(g)
                    self.inc(gb, g.dma_start(out=o_out1[ci * P:(ci + 1) * P, :], in_=outT[:]), 16)
                    if ci == 0:
                        self.inc(gb, g.dma_start(out=self.dbg_out("S1", [P, H * P]), in_=self.Ssb[:]), 16)
                    E["dbg1"] = self.inc(gb, g.dma_start(out=o_oq[ci * P:(ci + 1) * P, :], in_=oq_tm[:]), 16)
            emit_load(ci + 2)

        self.p2_ev = ev

    def phase_p3(self):
        b = self.block
        NCH, N = self.NCH, self.N
        sc3 = ExitStack()
        self.sc_p3 = sc3
        wf = [self.sb(f"p3wf{i}", [P, 16 * P], F32, scope=sc3) for i in range(2)]
        wq = [self.sb(f"p3wq{i}", [P, 16 * P], BF16, scope=sc3) for i in range(2)]
        evp = self.sb("p3ev", [P, C], F32, scope=sc3)
        xb = [self.sb(f"p3xb{i}", [P, P], F32, scope=sc3) for i in range(2)]
        x1p = [self.sb(f"p3x1{i}", [P, P], F32, scope=sc3) for i in range(2)]
        pbrep = self.sb("pbrep", [P, C], F32, scope=sc3)
        sc3r = ExitStack()
        ebr = self.replicate_rows_from_row(None, pbrep, pre_ev=None, scope=sc3r, src_cols=self.pbcol)
        sc3r.close()
        pmm = [self.ps("p3mm0", [P, C], F32, scope=sc3)]
        psT = self.ps("p3psT", [P, 1024], F32, scope=sc3)
        st = dict(wld=[], wq=[], mm=[], ev=[], tr={}, xld={}, piece={})

        if self.DBG:
            o_x1 = self.dbg_out("x1", [N, C])

        def pc2(oi, ci):
            pc = oi * NCH + ci - 2
            return None if pc < 0 else st["piece"].get((pc // NCH, pc % NCH))

        for oi in range(16):
            slot = oi % 2

            @b.sync
            def _(sy, oi=oi, slot=slot):
                if oi >= 2:
                    self.wt(sy, st["wq"][oi - 2])
                gb = self.dgrp(sy)
                e = None
                for it in range(16):
                    e = self.inc(gb, sy.dma_start(
                        out=wf[slot][:, it * P:(it + 1) * P],
                        in_=self.projT_d[it * P:(it + 1) * P, oi * P:(oi + 1) * P]), 16)
                st["wld"].append(e)

            @b.scalar
            def _(sc, oi=oi, slot=slot):
                self.wt(sc, st["wld"][oi])
                if oi >= 2:
                    self.wt(sc, st["mm"][oi - 2])
                st["wq"].append(self.inc(self.sSE, sc.activation(wq[slot][:], wf[slot][:], ACT.Sign)))

            @b.tensor
            def _(t, oi=oi, slot=slot):
                self.wt(t, st["wq"][oi])
                if oi == 0:
                    self.wt(t, ebr)
                if oi >= 1:
                    self.wt(t, st["ev"][oi - 1])
                pm = pmm[0]
                ins = None
                for it in range(16):
                    for w in range(self.NWIN):
                        ins = t.matmul(pm[:, self.WN * w:self.WN * (w + 1)],
                                       wq[slot][:, it * P:(it + 1) * P],
                                       self.big0[:, it * C + self.WN * w: it * C + self.WN * (w + 1)],
                                       start=(it == 0), stop=(it == 15))
                st["mm"].append(self.inc(self.sTE, ins))

            @b.vector
            def _(v, oi=oi, slot=slot):
                self.wt(v, st["mm"][oi])
                if oi >= 1:
                    self.wt(v, st["tr"][(oi - 1, NCH - 1)])
                e = self.inc(self.sVE, v.tensor_copy(evp[:, 0:N], pmm[0][:, 0:N]))
                self.wt(v, e)
                st["ev"].append(e)

            for ci in range(NCH):
                @b.tensor
                def _(t, oi=oi, ci=ci):
                    self.wt(t, st["ev"][oi], pc2(oi, ci))
                    sl = (ci % 2) * 512
                    ins = t.transpose(psT[:, sl:sl + P], evp[:, ci * P:(ci + 1) * P], self.idf[:])
                    st["tr"][(oi, ci)] = self.inc(self.sTE, ins)

                @b.sync
                def _(sy, oi=oi, ci=ci):
                    self.wt(sy, pc2(oi, ci))
                    gb = self.dgrp(sy)
                    st["xld"][(oi, ci)] = self.inc(gb, sy.dma_start(
                        out=xb[ci % 2][:], in_=self.x_d[ci * P:(ci + 1) * P, oi * P:(oi + 1) * P]), 16)

                @b.vector
                def _(v, oi=oi, ci=ci):
                    self.wt(v, st["tr"][(oi, ci)], st["xld"][(oi, ci)], pc2(oi, ci))
                    if oi == 0 and ci == 0:
                        self.wt(v, ebr)
                    sl = (ci % 2) * 512
                    e = self.inc(self.sVE, v.tensor_tensor(xb[ci % 2][:], xb[ci % 2][:],
                                                           pbrep[:, oi * P:(oi + 1) * P], ALU.add))
                    self.wt(v, e)
                    e = self.inc(self.sVE, v.scalar_tensor_tensor(
                        x1p[ci % 2][:], psT[:, sl:sl + P], self.al2T[:, ci:ci + 1],
                        xb[ci % 2][:], ALU.mult, ALU.add))
                    self.wt(v, e)
                    st[("x1v", oi, ci)] = e

                @b.scalar
                def _(sc, oi=oi, ci=ci):
                    self.wt(sc, st[("x1v", oi, ci)])
                    gb = self.dgrp(sc)
                    e = self.inc(gb, sc.dma_start(out=self.x1_d[ci * P:(ci + 1) * P, oi * P:(oi + 1) * P],
                                                  in_=x1p[ci % 2][:]), 16)
                    if self.DBG:
                        e = self.inc(gb, sc.dma_start(out=o_x1[ci * P:(ci + 1) * P, oi * P:(oi + 1) * P],
                                                      in_=x1p[ci % 2][:]), 16)
                    st["piece"][(oi, ci)] = e

        self.p3_last = st["piece"][(15, NCH - 1)]

    def phase_f1(self):
        b = self.block
        NCH, N, HOT = self.NCH, self.N, self.HOT
        sf = ExitStack()
        self.sc_f1 = sf
        x1c = [self.sb(f"f1x1{i}", [P, C], F32, scope=sf) for i in range(2)]
        xq3 = self.sb("f1xq3", [P, C], BF16, scope=sf)
        sfA = ExitStack()
        ptr = self.ps("f1ptr", [P, C], BF16, scope=sfA)
        gmax = self.sb("gmax", [P, N], F32, scope=sf)
        gmin = self.sb("gmin", [P, N], F32, scope=sf)
        nthw1 = self.sb("nthw1", [P, 1], F32, scope=sf)
        nthw2 = self.sb("nthw2", [P, 1], F32, scope=sf)
        ev = dict(ld={}, ve={}, sq={}, sqr={}, tr={}, fm={})

        if self.DBG:
            o_xq3 = self.dbg_out("xq3", [N, C])

        @b.sync
        def _(sy):
            gb = self.dgrp(sy)
            self.ev_gl = self.inc(gb, sy.dma_start(out=self.t1[0:1, 0:C], in_=self.garow_d), 16)

        @b.vector
        def _(v):
            e = self.inc(self.sVE, v.tensor_scalar(nthw1[:], self.thw1[:], -1.0, None, ALU.mult))
            e = self.inc(self.sVE, v.tensor_scalar(nthw2[:], self.thw2[:], -1.0, None, ALU.mult))
            e = self.inc(self.sVE, v.memset(gmax[:], -3.0e38))
            e = self.inc(self.sVE, v.memset(gmin[:], 3.0e38))
            self.wt(v, e)
            self.ev_f1init = e

        egr = self.replicate_rows_from_row(self.t1[0:1, 0:C], self.repA, pre_ev=self.ev_gl, scope=sfA)
        self.f1_garep = egr

        for ci in range(NCH):
            slot = ci % 2

            @b.sync
            def _(sy, ci=ci, slot=slot):
                if ci >= 2:
                    self.wt(sy, ev["ve"][ci - 2])
                gb = self.dgrp(sy)
                ev["ld"][ci] = self.inc(gb, sy.dma_start(out=x1c[slot][:], in_=self.x1_d[ci * P:(ci + 1) * P, :]), 16)

            @b.scalar
            def _(sc, ci=ci, slot=slot):
                self.wt(sc, ev["ld"][ci])
                if ci >= 1:
                    self.wt(sc, ev[("xq3", ci - 1)])
                ev["sq"][ci] = self.inc(self.sSE, sc.activation(self.t0[:], x1c[slot][:], ACT.Square, accum_out=self.s_b[:]))

            @b.vector
            def _(v, ci=ci, slot=slot):
                self.wt(v, ev["ld"][ci], ev["sq"][ci], self.f1_garep, self.ev_f1init)
                e = self.inc(self.sVE, v.reduce_sum(self.s_a[:], x1c[slot][:], AX))
                self.wt(v, e)
                e = self.inc(self.sVE, v.tensor_scalar(self.s_a[:], self.s_a[:], 1.0 / C, None, ALU.mult))
                self.wt(v, e)
                e = self.inc(self.sVE, v.tensor_tensor(self.s_c[:], self.s_a[:], self.s_a[:], ALU.mult))
                self.wt(v, e)
                e = self.inc(self.sVE, v.scalar_tensor_tensor(self.s_c[:], self.s_b[:], 1.0 / C, self.s_c[:], ALU.mult, ALU.subtract))
                self.wt(v, e)
                e = self.inc(self.sVE, v.tensor_scalar(self.s_c[:], self.s_c[:], 1.0 + EPS2, EPS1 * EPS2, ALU.mult, ALU.add))
                self.wt(v, e)
                ev[("var", ci)] = e

            @b.scalar
            def _(sc, ci=ci):
                self.wt(sc, ev[("var", ci)])
                ev["sqr"][ci] = self.inc(self.sSE, sc.activation(self.s_d[:], self.s_c[:], ACT.Sqrt))

            @b.vector
            def _(v, ci=ci, slot=slot):
                self.wt(v, ev["sqr"][ci])
                e = self.inc(self.sVE, v.reciprocal(self.s_d[:], self.s_d[:]))
                self.wt(v, e)
                e = self.inc(self.sVE, v.tensor_scalar(self.t0[:], x1c[slot][:], self.s_a[:, 0:1], self.s_d[:, 0:1], ALU.subtract, ALU.mult))
                ev["ve"][ci] = e
                self.wt(v, e)
                e = self.inc(self.sVE, v.tensor_tensor(self.t0[:], self.t0[:], self.repA[:], ALU.mult))
                self.wt(v, e)
                e = self.inc(self.sVE, v.reduce_max(self.s_e[:], self.t0[:], AX, apply_absolute_value=True))
                self.wt(v, e)
                e = self.inc(self.sVE, v.tensor_scalar(self.s_e[:], self.s_e[:], 1e-5, None, ALU.max))
                self.wt(v, e)
                e = self.inc(self.sVE, v.tensor_scalar(self.s_f[:], self.s_e[:], self.wsw1[:, 0:1], 0.5 / 127.0, ALU.mult, ALU.mult))
                self.wt(v, e)
                e = self.inc(self.sVE, v.tensor_copy(self.al1T[:, ci:ci + 1], self.s_f[:]))
                e = self.inc(self.sVE, v.reciprocal(self.s_e[:], self.s_e[:]))
                self.wt(v, e)
                e = self.inc(self.sVE, v.tensor_scalar(self.s_e[:], self.s_e[:], 127.0, None, ALU.mult))
                self.wt(v, e)
                if ci >= 1:
                    self.wt(v, ev["tr"][ci - 1])
                    self.wt(v, ev.get(("dbg3", ci - 1)))
                e = self.inc(self.sVE, v.tensor_scalar(self.t1[:], self.t0[:], self.s_e[:, 0:1], M23, ALU.mult, ALU.add))
                self.wt(v, e)
                e = self.inc(self.sVE, v.tensor_scalar(self.t1[:], self.t1[:], M23, None, ALU.subtract))
                self.wt(v, e)
                e = self.inc(self.sVE, v.tensor_scalar(self.t0[:], self.t1[:], M52, M52, ALU.add, ALU.subtract))
                self.wt(v, e)
                e = self.inc(self.sVE, v.scalar_tensor_tensor(xq3[:], self.t1[:], self.cccol[:, 0:1], self.t0[:], ALU.mult, ALU.add))
                self.wt(v, e)
                ev[("xq3", ci)] = e

            @b.tensor
            def _(t, ci=ci):
                self.wt(t, ev[("xq3", ci)])
                if ci >= 1:
                    self.wt(t, ev["fm"][ci - 1])
                ins = None
                for idx in range(16):
                    ins = t.transpose(ptr[:, P * idx:P * (idx + 1)], xq3[:, P * idx:P * (idx + 1)], self.idb[:])
                ev["tr"][ci] = self.inc(self.sTE, ins)

            @b.vector
            def _(v, ci=ci):
                self.wt(v, ev["tr"][ci])
                e = None
                for idx in range(16):
                    e = self.inc(self.sVE, v.tensor_copy(
                        self.big0[:, idx * C + ci * P: idx * C + (ci + 1) * P],
                        ptr[:, P * idx:P * (idx + 1)]))
                ev["fm"][ci] = e

            if self.DBG:
                @b.gpsimd
                def _(g, ci=ci):
                    self.wt(g, ev[("xq3", ci)])
                    gb = self.dgrp(g)
                    ev[("dbg3", ci)] = self.inc(gb, g.dma_start(out=o_xq3[ci * P:(ci + 1) * P, :], in_=xq3[:]), 16)

        e2 = self.cols_to_row(self.al1T, self.tokrow, pre_ev=ev[("xq3", NCH - 1)], scope=sfA)
        e2 = self.replicate_rows(self.tokrow, self.repA, pre_ev=e2, scope=sfA)
        self.f1_al3rep = e2
        sfA.close()

        w1f = [self.sb(f"f1wf{i}", [P, 16 * P], F32, scope=sf) for i in range(2)]
        w1qa = [self.sb(f"f1wqa{i}", [P, 16 * P], BF16, scope=sf) for i in range(2)]
        w1qb = [self.sb(f"f1wqb{i}", [P, 16 * P], BF16, scope=sf) for i in range(2)]
        gbuf = [self.sb(f"f1g{i}", [P, C], F32, scope=sf) for i in range(2)]
        sfB = ExitStack()
        pm1 = [self.ps(f"f1mm{i}", [P, C], F32, scope=sfB) for i in range(2)]
        stm = dict(wld=[], wq=[], mm=[], deq=[], gl=[], mx=[], gst=[])

        for oi in range(HOT):
            slot = oi % 2

            @b.sync
            def _(sy, oi=oi, slot=slot):
                if oi >= 2:
                    self.wt(sy, stm["wq"][oi - 2])
                gb = self.dgrp(sy)
                e = None
                for it in range(16):
                    e = self.inc(gb, sy.dma_start(
                        out=w1f[slot][:, it * P:(it + 1) * P],
                        in_=self.w1T_d[it * P:(it + 1) * P, oi * P:(oi + 1) * P]), 16)
                stm["wld"].append(e)

            @b.scalar
            def _(sc, oi=oi, slot=slot):
                self.wt(sc, stm["wld"][oi], self.ev_f1init)
                if oi >= 2:
                    self.wt(sc, stm["mm"][oi - 2])
                e = self.inc(self.sSE, sc.activation(w1qa[slot][:], w1f[slot][:], ACT.Sign, bias=nthw1[:, 0:1]))
                stm[("sgA", oi)] = e
                e = self.inc(self.sSE, sc.activation(w1qb[slot][:], w1f[slot][:], ACT.Sign, bias=self.thw1[:, 0:1]))
                stm[("sgB", oi)] = e

            @b.vector
            def _(v, oi=oi, slot=slot):
                self.wt(v, stm[("sgB", oi)])
                if oi >= 2:
                    self.wt(v, stm["mm"][oi - 2])
                e = self.inc(self.sVE, v.tensor_tensor(w1qa[slot][:], w1qa[slot][:], w1qb[slot][:], ALU.add))
                self.wt(v, e)
                stm["wq"].append(e)

            @b.tensor
            def _(t, oi=oi, slot=slot):
                self.wt(t, stm["wq"][oi], ev["fm"][NCH - 1], self.f1_al3rep)
                if oi >= 2:
                    self.wt(t, stm["deq"][oi - 2])
                pm = pm1[slot]
                ins = None
                for it in range(16):
                    for w in range(self.NWIN):
                        ins = t.matmul(pm[:, self.WN * w:self.WN * (w + 1)],
                                       w1qa[slot][:, it * P:(it + 1) * P],
                                       self.big0[:, it * C + self.WN * w: it * C + self.WN * (w + 1)],
                                       start=(it == 0), stop=(it == 15))
                stm["mm"].append(self.inc(self.sTE, ins))

            @b.vector
            def _(v, oi=oi, slot=slot):
                self.wt(v, stm["mm"][oi])
                if oi >= 2:
                    self.wt(v, stm["gst"][oi - 2])
                e = self.inc(self.sVE, v.tensor_tensor(gbuf[slot][:, 0:N], pm1[slot][:, 0:N], self.repA[:, 0:N], ALU.mult))
                self.wt(v, e)
                stm["deq"].append(e)

            @b.scalar
            def _(sc, oi=oi, slot=slot):
                self.wt(sc, stm["deq"][oi])
                e = self.inc(self.sSE, sc.activation(gbuf[slot][:, 0:N], gbuf[slot][:, 0:N], ACT.Gelu))
                stm["gl"].append(e)

            @b.vector
            def _(v, oi=oi, slot=slot):
                self.wt(v, stm["gl"][oi])
                if oi == 0:
                    self.wt(v, self.ev_f1init)
                e = self.inc(self.sVE, v.tensor_tensor(gmax[:], gmax[:], gbuf[slot][:, 0:N], ALU.max))
                e = self.inc(self.sVE, v.tensor_tensor(gmin[:], gmin[:], gbuf[slot][:, 0:N], ALU.min))
                self.wt(v, e)
                stm["mx"].append(e)

            @b.scalar
            def _(sc, oi=oi, slot=slot):
                self.wt(sc, stm["mx"][oi])
                gb = self.dgrp(sc)
                stm["gst"].append(self.inc(gb, sc.dma_start(out=self.g_d[oi][:, 0:N], in_=gbuf[slot][:, 0:N]), 16))

        @b.vector
        def _(v):
            self.wt(v, stm["mx"][-1])
            e = self.inc(self.sVE, v.tensor_scalar(gmin[:], gmin[:], -1.0, None, ALU.mult))
            self.wt(v, e)
            e = self.inc(self.sVE, v.tensor_tensor(gmax[:], gmax[:], gmin[:], ALU.max))
            self.wt(v, e)
            self.f1_mx = e

        sfB.close()
        sfC = ExitStack()
        pxm = self.ps("f1pxm", [P, C], F32, scope=sfC)

        @b.tensor
        def _(t):
            self.wt(t, self.f1_mx, stm["deq"][-1])
            ins = None
            for ci in range(NCH):
                ins = t.transpose(pxm[:, (ci % 16) * P:(ci % 16 + 1) * P], gmax[:, ci * P:(ci + 1) * P], self.idf[:])
            self.f1_mxT = self.inc(self.sTE, ins)

        @b.vector
        def _(v):
            self.wt(v, self.f1_mxT)
            e = None
            for ci in range(NCH):
                e = self.inc(self.sVE, v.reduce_max(self.am4T[:, ci:ci + 1], pxm[:, (ci % 16) * P:(ci % 16 + 1) * P], AX))
            self.wt(v, e)
            e = self.inc(self.sVE, v.tensor_scalar(self.am4T[:], self.am4T[:], 1e-5, None, ALU.max))
            self.wt(v, e)
            e = self.inc(self.sVE, v.tensor_scalar(self.al4T[:], self.am4T[:], self.wsw2[:, 0:1], 0.5 / 127.0, ALU.mult, ALU.mult))
            self.wt(v, e)
            e = self.inc(self.sVE, v.reciprocal(self.sc1T[:], self.am4T[:]))
            self.wt(v, e)
            e = self.inc(self.sVE, v.tensor_scalar(self.sc1T[:], self.sc1T[:], 127.0, None, ALU.mult))
            self.wt(v, e)
            self.f1_sc4 = e

        e2 = self.cols_to_row(self.sc1T, self.tokrow, pre_ev=self.f1_sc4, scope=sfC)
        e2 = self.replicate_rows(self.tokrow, self.repA, pre_ev=e2, scope=sfC)
        self.f1_sc4rep = e2
        sfC.close()

        stw = dict(wld=[], sg=[], add=[], st=[])
        for oi in range(HOT):
            slot = oi % 2

            @b.sync
            def _(sy, oi=oi, slot=slot):
                if oi >= 2:
                    self.wt(sy, stw["add"][oi - 2])
                else:
                    self.wt(sy, stm[("sgB", HOT - 2 + oi)])
                gb = self.dgrp(sy)
                stw["wld"].append(self.inc(gb, sy.dma_start(
                    out=w1f[slot][:, 0:C], in_=self.w2T_d[oi * P:(oi + 1) * P, :]), 16))

            @b.scalar
            def _(sc, oi=oi, slot=slot):
                self.wt(sc, stw["wld"][oi])
                if oi >= 2:
                    self.wt(sc, stw["st"][oi - 2])
                else:
                    self.wt(sc, stm["mm"][HOT - 2 + oi], stm["wq"][HOT - 2 + oi])
                e = self.inc(self.sSE, sc.activation(w1qa[slot][:, 0:C], w1f[slot][:, 0:C], ACT.Sign, bias=nthw2[:, 0:1]))
                e = self.inc(self.sSE, sc.activation(w1qb[slot][:, 0:C], w1f[slot][:, 0:C], ACT.Sign, bias=self.thw2[:, 0:1]))
                stw["sg"].append(e)

            @b.vector
            def _(v, oi=oi, slot=slot):
                self.wt(v, stw["sg"][oi])
                e = self.inc(self.sVE, v.tensor_tensor(w1qa[slot][:, 0:C], w1qa[slot][:, 0:C], w1qb[slot][:, 0:C], ALU.add))
                self.wt(v, e)
                stw["add"].append(e)

            @b.scalar
            def _(sc, oi=oi, slot=slot):
                self.wt(sc, stw["add"][oi])
                gb = self.dgrp(sc)
                stw["st"].append(self.inc(gb, sc.dma_start(out=self.w2q_d[oi * P:(oi + 1) * P, :], in_=w1qa[slot][:, 0:C]), 16))
        self.f1_w2q = stw["st"][-1]

    def phase_f2(self):
        b = self.block
        NCH, N, HOT = self.NCH, self.N, self.HOT
        sf2 = ExitStack()
        self.sc_f2 = sf2
        NH = min(512, N)
        nhalves = max(1, N // NH)
        hqbuf = self.sb("hqbuf", [P, HOT * NH], BF16, scope=sf2)
        gld = [self.sb(f"f2g{i}", [P, NH], F32, scope=sf2) for i in range(2)]
        wsl = [self.sb(f"f2w{i}", [P, HOT * P], BF16, scope=sf2) for i in range(2)]
        x1b = [self.sb(f"f2x1{i}", [P, P], F32, scope=sf2) for i in range(2)]
        yp = [self.sb(f"f2y{i}", [P, P], F32, scope=sf2) for i in range(2)]
        t0h = self.sb("f2t0", [P, NH], F32, scope=sf2)
        t1h = self.sb("f2t1", [P, NH], F32, scope=sf2)
        o2sb = [self.sb(f"f2o2{i}", [P, NH], F32, scope=sf2) for i in range(2)]
        pm2 = [self.ps(f"f2mm{i}", [P, NH], F32, scope=sf2) for i in range(2)]
        psT2 = self.ps("f2psT", [P, 1024], F32, scope=sf2)
        st = dict(gld={}, hq={}, wld={}, mm={}, ev={}, tr={}, xld={}, yst={})
        NT = NH // P

        for hf in range(nhalves):
            for oi in range(HOT):
                slot = oi % 2

                @b.sync
                def _(sy, hf=hf, oi=oi, slot=slot):
                    k = hf * HOT + oi
                    if k >= 2:
                        self.wt(sy, st["hq"].get(k - 2))
                    gb = self.dgrp(sy)
                    st["gld"][k] = self.inc(gb, sy.dma_start(
                        out=gld[slot][:], in_=self.g_d[oi][:, hf * NH:(hf + 1) * NH]), 16)

                @b.vector
                def _(v, hf=hf, oi=oi, slot=slot):
                    k = hf * HOT + oi
                    self.wt(v, st["gld"][k], self.f1_sc4rep, self.f1_w2q)
                    if hf > 0 and oi == 0:
                        self.wt(v, st["evlast"])
                    e = self.inc(self.sVE, v.tensor_tensor(t0h[:], gld[slot][:], self.repA[:, hf * NH:(hf + 1) * NH], ALU.mult))
                    self.wt(v, e)
                    e = self.inc(self.sVE, v.tensor_scalar(t0h[:], t0h[:], M23, None, ALU.add))
                    self.wt(v, e)
                    e = self.inc(self.sVE, v.tensor_scalar(t0h[:], t0h[:], M23, None, ALU.subtract))
                    self.wt(v, e)
                    e = self.inc(self.sVE, v.tensor_scalar(t1h[:], t0h[:], M52, M52, ALU.add, ALU.subtract))
                    self.wt(v, e)
                    e = self.inc(self.sVE, v.scalar_tensor_tensor(
                        hqbuf[:, oi * NH:(oi + 1) * NH], t0h[:], self.cccol[:, 0:1], t1h[:], ALU.mult, ALU.add))
                    self.wt(v, e)
                    st["hq"][k] = e

            for cti in range(16):
                slot = cti % 2

                @b.sync
                def _(sy, hf=hf, cti=cti, slot=slot):
                    kk = hf * 16 + cti
                    if kk >= 2:
                        self.wt(sy, st["mm"][kk - 2])
                    gb = self.dgrp(sy)
                    st["wld"][kk] = self.inc(gb, sy.dma_start(
                        out=wsl[slot].rearrange("p (o q) -> p o q", o=HOT),
                        in_=self.w2q_d[:, cti * P:(cti + 1) * P].rearrange("(o p) q -> p o q", p=P)), 16)

                @b.tensor
                def _(t, hf=hf, cti=cti, slot=slot):
                    kk = hf * 16 + cti
                    self.wt(t, st["wld"][kk], st["hq"][hf * HOT + HOT - 1])
                    if kk >= 2:
                        self.wt(t, st["ev"][kk - 2])
                    pm = pm2[slot]
                    ins = None
                    for oi in range(HOT):
                        for w in range(NH // 512 if NH >= 512 else 1):
                            wn = min(512, NH)
                            ins = t.matmul(pm[:, wn * w:wn * (w + 1)],
                                           wsl[slot][:, oi * P:(oi + 1) * P],
                                           hqbuf[:, oi * NH + wn * w: oi * NH + wn * (w + 1)],
                                           start=(oi == 0), stop=(oi == HOT - 1))
                    st["mm"][kk] = self.inc(self.sTE, ins)
                    if hf == nhalves - 1 and cti == 15:
                        st["evlast2"] = st["mm"][kk]
                    if cti == 15:
                        st["evlast"] = st["mm"][kk]

                @b.vector
                def _(v, hf=hf, cti=cti, slot=slot):
                    kk = hf * 16 + cti
                    self.wt(v, st["mm"][kk])
                    if kk >= 2:
                        self.wt(v, st["tr"].get((kk - 2, NT - 1)))
                    e = self.inc(self.sVE, v.tensor_copy(o2sb[slot][:], pm2[slot][:]))
                    self.wt(v, e)
                    st["ev"][kk] = e

                for tt_ in range(NT):
                    ci = (hf * NH + tt_ * P) // P

                    @b.tensor
                    def _(t, hf=hf, cti=cti, slot=slot, tt_=tt_, ci=ci):
                        kk = hf * 16 + cti
                        self.wt(t, st["ev"][kk])
                        key = (kk, tt_)
                        pkey = (kk, tt_ - 2) if tt_ >= 2 else (kk - 1, NT - 2 + tt_)
                        if pkey in st["yst"]:
                            self.wt(t, st["yst"][pkey])
                        ins = t.transpose(psT2[:, (tt_ % 2) * 512:(tt_ % 2) * 512 + P],
                                          o2sb[slot][:, tt_ * P:(tt_ + 1) * P], self.idf[:])
                        st["tr"][key] = self.inc(self.sTE, ins)

                    @b.sync
                    def _(sy, hf=hf, cti=cti, tt_=tt_, ci=ci):
                        key = (hf * 16 + cti, tt_)
                        pkey = (hf * 16 + cti, tt_ - 2) if tt_ >= 2 else (hf * 16 + cti - 1, NT - 2 + tt_)
                        if pkey in st["yst"]:
                            self.wt(sy, st["yst"][pkey])
                        gb = self.dgrp(sy)
                        st["xld"][key] = self.inc(gb, sy.dma_start(
                            out=x1b[tt_ % 2][:], in_=self.x1_d[ci * P:(ci + 1) * P, cti * P:(cti + 1) * P]), 16)

                    @b.vector
                    def _(v, hf=hf, cti=cti, tt_=tt_, ci=ci):
                        key = (hf * 16 + cti, tt_)
                        self.wt(v, st["tr"][key], st["xld"][key])
                        e = self.inc(self.sVE, v.scalar_tensor_tensor(
                            yp[tt_ % 2][:], psT2[:, (tt_ % 2) * 512:(tt_ % 2) * 512 + P],
                            self.al4T[:, ci:ci + 1], x1b[tt_ % 2][:], ALU.mult, ALU.add))
                        self.wt(v, e)
                        st[("yv", key)] = e

                    @b.scalar
                    def _(sc, hf=hf, cti=cti, tt_=tt_, ci=ci):
                        key = (hf * 16 + cti, tt_)
                        self.wt(sc, st[("yv", key)])
                        gb = self.dgrp(sc)
                        st["yst"][key] = self.inc(gb, sc.dma_start(
                            out=self.y_d[ci * P:(ci + 1) * P, cti * P:(cti + 1) * P], in_=yp[tt_ % 2][:]), 16)

        self.f2_done = list(st["yst"].values())

    def finalize(self):
        b = self.block

        @b.sync
        def _(sy):
            for e in self.f2_done[-32:]:
                self.wt(sy, e)
            for s in self.dma_pool + self.gp_pool:
                if s["n"] > 0:
                    self.wt(sy, self.ev(s))

    def build(self):
        self.declare()
        self.alloc()
        self.phase_consts()
        if self.PHASES >= 1:
            self.phase_wmeans()
        if self.PHASES >= 2:
            self.phase_p0()
            self.barrier("p0")
            if self.PHASES >= 1:
                self.sc_w.close()
        if self.PHASES >= 3:
            self.phase_p1()
            self.barrier("p1")
        if self.PHASES >= 4:
            self.phase_p2()
            self.barrier("p2")
            self.sc_p2.close()
        if self.PHASES >= 5:
            self.phase_p3()
            self.barrier("p3")
            self.sc_p3.close()
        if self.PHASES >= 6:
            self.phase_f1()
            self.barrier("f1")
            self.sc_f1.close()
        if self.PHASES >= 7:
            self.sc_big0.close()
            self.phase_f2()
            self.finalize()
            self.sc_f2.close()
            return self.nc
        self.finalize_stub()
        return self.nc

    def finalize_stub(self):
        b = self.block
        sc = ExitStack()
        tb = self.sb("fin_tb", [P, C], F32, scope=sc)

        @b.sync
        def _(sy):
            for ci in range(self.NCH):
                gb = self.dgrp(sy)
                e = self.inc(gb, sy.dma_start(out=tb[:], in_=self.x_d[ci * P:(ci + 1) * P, :]), 16)
                self.wt(sy, e)
                gb = self.dgrp(sy)
                e = self.inc(gb, sy.dma_start(out=self.y_d[ci * P:(ci + 1) * P, :], in_=tb[:]), 16)
                self.wt(sy, e)
        sc.close()


def _host_prep(inputs):
    f = np.float32
    bf = ml_dtypes.bfloat16
    x = np.ascontiguousarray(np.asarray(inputs["L0_attn_x"], f))
    pr = {
        "qkvT": np.ascontiguousarray(np.asarray(inputs["L0_attn_qkv_w"], f).T),
        "gateT": np.ascontiguousarray(np.asarray(inputs["L0_attn_gate_w"], f).T),
        "projT": np.ascontiguousarray(np.asarray(inputs["L0_attn_proj_w"], f).T),
        "w1T": np.ascontiguousarray(np.asarray(inputs["L0_ffn_w1"], f).T),
        "w2T": np.ascontiguousarray(np.asarray(inputs["L0_ffn_w2"], f).T),
        "qkvb": np.asarray(inputs["L0_attn_qkv_b"], f).reshape(1, -1),
        "gateb": np.asarray(inputs["L0_attn_gate_b"], f).reshape(1, -1),
        "projb": np.asarray(inputs["L0_attn_proj_b"], f).reshape(1, -1),
        "nwrow": np.asarray(inputs["L0_attn_norm_w"], f).reshape(1, -1),
        "gammarow": np.asarray(inputs["L0_ffn_gamma"], f).reshape(1, -1),
        "identb": np.eye(P, dtype=bf),
        "identf": np.eye(P, dtype=np.float32),
    }
    mj = np.zeros((1, 256), bf)
    mj[0, :128] = bf(-B48)
    mj[0, 128:] = bf(B48)
    mb = np.zeros((8, 256), bf)
    for r in range(128):
        mb[r // 16, r] = bf(B48)
        mb[r // 16, 128 + r] = bf(B48)
    pr["maskJ"] = mj
    pr["maskB"] = mb
    return x, pr


def kernel(**inputs):
    x, pr = _host_prep(inputs)
    kb = KB(CFG)
    nc = kb.build()
    in_maps = []
    for bi in range(8):
        m = {"x": np.ascontiguousarray(x[bi])}
        m.update(pr)
        in_maps.append(m)
    res = run_bass_kernel_spmd(nc, in_maps, core_ids=list(range(8)))
    out = np.stack([res.results[bi]["y"] for bi in range(8)], axis=0)
    return out.astype(np.float32)
